# version 06
# baseline (speedup 1.0000x reference)
from contextlib import ExitStack
import numpy as np
import ml_dtypes
import concourse.bass as bass
import concourse.mybir as mybir
from concourse.bass_utils import run_bass_kernel_spmd

F32 = mybir.dt.float32
BF16 = mybir.dt.bfloat16
I32 = mybir.dt.int32
AF = mybir.ActivationFunctionType
ALU = mybir.AluOpType
NPBF = ml_dtypes.bfloat16

D = 1024
T = 4096
OT = 2048
NCH = 8
NKT = 32
NQB = 4
NTB = 8
EPS = 1e-5
SAFETY = 1.1025 / 64.0


class Sched:
    DSEMS = ["XT0", "XT1", "XS0", "XS1", "RT0", "RT1", "M2", "BM",
             "WS0", "WS1", "SW0", "SW1", "DM", "OS0", "OS1",
             "WL0", "WL1", "C0", "SLD", "LLD", "WST", "STST", "LBST",
             "WOT", "OB", "XL0", "XL1", "X20", "X21", "HS0", "HS1",
             "HLD", "LB0", "LB1", "QB0", "QB1", "SB0", "SB1", "O50", "O51"]

    def __init__(self):
        self.n = {"pe": 0, "act": 0, "dve": 0, "sy": 0, "gp": 0}
        for d in self.DSEMS:
            self.n[d] = 0
        self.m = {}
        self.em = False
        self.sem = None
        self.cur = None
        self.trace = {}

    def snap(self):
        return dict(self.n)

    def restore(self, s):
        self.n = dict(s)

    def mark(self, eng, key, instr, dma=False):
        self.n[eng] += 16 if dma else 1
        v = self.n[eng]
        if self.em:
            assert instr is not None, (eng, key)
            instr.then_inc(self.sem[eng], 16 if dma else 1)
            assert self.m.get((eng,) + key) == v, ("mark mismatch", eng, key)
            self.trace.setdefault(self.cur, []).append(("i", eng, 16 if dma else 1, key))
        self.m[(eng,) + key] = v
        return v

    def bump(self, eng, instr=None, dma=False):
        self.n[eng] += 16 if dma else 1
        if self.em and instr is not None:
            instr.then_inc(self.sem[eng], 16 if dma else 1)
            self.trace.setdefault(self.cur, []).append(("i", eng, 16 if dma else 1, None))
        return self.n[eng]

    def val(self, eng, key):
        return self.m.get((eng,) + key, 0)

    def wait(self, obj, eng, v):
        if self.em and v > 0:
            obj.wait_ge(self.sem[eng], v)
            self.trace.setdefault(self.cur, []).append(("w", eng, v, None))

    def wk(self, obj, eng, key):
        self.wait(obj, eng, self.val(eng, key))

    def wt(self, obj, eng):
        self.wait(obj, eng, self.n[eng])


_SCHED_LAST = None


def build_nc(debug=False):
    global _SCHED_LAST
    KIND = "ExternalOutput" if debug else "Internal"
    nc = bass.Bass("TRN2", target_bir_lowering=False)

    x_in = nc.dram_tensor("x_in", [T, D], F32, kind="ExternalInput").ap()
    wT_in = nc.dram_tensor("wT_in", [4, NCH, 128, D], F32, kind="ExternalInput").ap()
    cols_in = nc.dram_tensor("cols_in", [128, 128], F32, kind="ExternalInput").ap()
    bv_in = nc.dram_tensor("bv_in", [1, D], F32, kind="ExternalInput").ap()
    ropeC = nc.dram_tensor("ropeC", [128, T], BF16, kind="ExternalInput").ap()
    ropeS = nc.dram_tensor("ropeS", [128, T], BF16, kind="ExternalInput").ap()
    identf_in = nc.dram_tensor("identf_in", [128, 128], F32, kind="ExternalInput").ap()
    emat_in = nc.dram_tensor("emat_in", [8, 128], BF16, kind="ExternalInput").ap()
    qwT_in = nc.dram_tensor("qwT_in", [6, NCH, 128, D], I32, kind="ExternalInput").ap()
    sT_in = nc.dram_tensor("sT_in", [6, 8, NCH * D], F32, kind="ExternalInput").ap()
    laT_in = nc.dram_tensor("laT_in", [6, NCH, 128, 32], F32, kind="ExternalInput").ap()
    lbTa_in = nc.dram_tensor("lbTa_in", [6, 33, D], F32, kind="ExternalInput").ap()
    out_d = nc.dram_tensor("out", [OT, D], F32, kind="ExternalOutput").ap()

    w_bf = nc.dram_tensor("w_bf", [4, NCH, 128, D], BF16, kind=KIND).ap()
    xscr = nc.dram_tensor("xscr", [NCH, 128, OT], F32, kind=KIND).ap()
    x2scr = nc.dram_tensor("x2scr", [NCH, 128, OT], F32, kind=KIND).ap()
    oscr = nc.dram_tensor("oscr", [NCH, 128, OT], BF16, kind=KIND).ap()
    hscr = nc.dram_tensor("hscr", [NCH, 128, OT], F32, kind=KIND).ap()
    sT_bf = nc.dram_tensor("sT_bf", [6, 8, NCH * D], BF16, kind="Internal").ap()
    lbTa_bf = nc.dram_tensor("lbTa_bf", [6, 33, D], BF16, kind="Internal").ap()
    laT_bf = nc.dram_tensor("laT_bf", [128, 1536], BF16, kind="Internal").ap()
    if debug:
        dbg_q = nc.dram_tensor("dbg_q", [128, NCH * OT], BF16, kind="ExternalOutput").ap()
        dbg_k = nc.dram_tensor("dbg_k", [128, NCH * T], BF16, kind="ExternalOutput").ap()
        dbg_v = nc.dram_tensor("dbg_v", [128, NKT * D], BF16, kind="ExternalOutput").ap()
        dbg_b = nc.dram_tensor("dbg_b", [128, 8], F32, kind="ExternalOutput").ap()
        dbg_ln1 = nc.dram_tensor("dbg_ln1", [NTB, 128, 4096], BF16, kind="ExternalOutput").ap()
        dbg_y = nc.dram_tensor("dbg_y", [128, 16384], BF16, kind="ExternalOutput").ap()
        dbg_g = nc.dram_tensor("dbg_g", [128, 16384], BF16, kind="ExternalOutput").ap()
        dbg_h1 = nc.dram_tensor("dbg_h1", [128, 16384], F32, kind="ExternalOutput").ap()
        dbg_w = nc.dram_tensor("dbg_w", [128, 8192], BF16, kind="ExternalOutput").ap()
        dbg_u = nc.dram_tensor("dbg_u", [33, 1024], BF16, kind="ExternalOutput").ap()

    S = Sched()
    _SCHED_LAST = S

    def two_pass(blk, bodies):
        snap = S.snap()
        S.em = False
        for _, fn in bodies:
            fn(None)
        end = S.snap()
        S.restore(snap)
        S.em = True
        eng_of = {"tensor": "pe", "scalar": "act", "vector": "dve",
                  "sync": "sy", "gpsimd": "gp"}
        for name, fn in bodies:
            S.cur = eng_of[name]
            getattr(blk, name)(fn)
        for e in ("pe", "act", "dve", "sy", "gp"):
            S.trace.setdefault(e, []).append(("b", None, 0, None))
        assert S.n == end, ("pass mismatch", S.n, end)

    _es = ExitStack()
    with _es:
        pe_q = _es.enter_context(nc.semaphore("pe_q"))
        act_q = _es.enter_context(nc.semaphore("act_q"))
        dve_q = _es.enter_context(nc.semaphore("dve_q"))
        sy_q = _es.enter_context(nc.semaphore("sy_q"))
        gp_q = _es.enter_context(nc.semaphore("gp_q"))
        identf = _es.enter_context(nc.sbuf_tensor("identf", [128, 128], F32))
        cols = _es.enter_context(nc.sbuf_tensor("cols", [128, 128], F32))
        onesb = _es.enter_context(nc.sbuf_tensor("onesb", [128, 8], BF16))
        emat = _es.enter_context(nc.sbuf_tensor("emat", [8, 128], BF16))
        bvrow = _es.enter_context(nc.sbuf_tensor("bvrow", [1, D], BF16))
        rows = _es.enter_context(nc.sbuf_tensor("rows", [128, 1024], F32))
        rows2 = _es.enter_context(nc.sbuf_tensor("rows2", [128, 1024], F32))
        rowsb = _es.enter_context(nc.sbuf_tensor("rowsb", [128, 1024], BF16))
        biasM = _es.enter_context(nc.sbuf_tensor("biasM", [128, 8], F32))
        nqmx = _es.enter_context(nc.sbuf_tensor("nqmx", [128, 8], F32))
        nkmx = _es.enter_context(nc.sbuf_tensor("nkmx", [128, 8], F32))
        ps_a = _es.enter_context(nc.psum_tensor("ps_a", [128, 2048], F32))
        ps_b = _es.enter_context(nc.psum_tensor("ps_b", [128, 2048], F32))
        S.sem = {"pe": pe_q, "act": act_q, "dve": dve_q, "sy": sy_q, "gp": gp_q}
        for _dn in Sched.DSEMS:
            S.sem[_dn] = _es.enter_context(nc.semaphore("d_" + _dn))

        def bankA(i):
            return ps_a[:, i * 512:(i + 1) * 512]

        def bankB(i):
            return ps_b[:, i * 512:(i + 1) * 512]

        def xbank(ch):
            return bankA(ch) if ch < 4 else bankB(ch - 4)

        ones_col = onesb[:, 0:1]
        ones_r128 = onesb[0:1, 0:1].broadcast_to([1, 128])
        ones_r64 = onesb[0:1, 0:1].broadcast_to([1, 64])

        col_bq = lambda ch: cols[:, ch:ch + 1]
        col_bk = lambda ch: cols[:, 8 + ch:9 + ch]
        col_bo = lambda ch: cols[:, 16 + ch:17 + ch]
        col_g = lambda b, ch: cols[:, 24 + 8 * b + ch:25 + 8 * b + ch]
        col_be = lambda b, ch: cols[:, 48 + 8 * b + ch:49 + 8 * b + ch]

        _e0 = ExitStack()
        with _e0:
            p0f = _e0.enter_context(nc.sbuf_tensor("p0f", [128, 4608], F32))
            p0b = _e0.enter_context(nc.sbuf_tensor("p0b", [128, 2048], BF16))
            p0s = _e0.enter_context(nc.sbuf_tensor("p0s", [8, 8192], F32))
            p0sb = _e0.enter_context(nc.sbuf_tensor("p0sb", [8, 8192], BF16))
            p0l = _e0.enter_context(nc.sbuf_tensor("p0l", [33, 1024], F32))
            p0lb = _e0.enter_context(nc.sbuf_tensor("p0lb", [33, 1024], BF16))
            p0c = _e0.enter_context(nc.sbuf_tensor("p0c", [128, 1536], BF16))
            b0 = _e0.enter_context(nc.Block())
            def p0_gp(g):
                em = S.em
                S.bump("C0", em and g.dma_start(identf[:], identf_in), dma=True)
                S.bump("C0", em and g.dma_start(cols[:], cols_in), dma=True)
                S.bump("C0", em and g.dma_start(emat[:], emat_in), dma=True)
                S.bump("C0", em and g.dma_start(p0f[0:1, 3584:3584 + D], bv_in), dma=True)
                for i in range(6):
                    ins = em and g.dma_start(
                        p0f[:, 2048 + i * 256:2048 + (i + 1) * 256].rearrange(
                            "p (c r) -> p c r", c=8),
                        laT_in[i].rearrange("c p r -> p c r"))
                    if i == 5:
                        S.mark("C0", ("consts",), ins, dma=True)
                    else:
                        S.bump("C0", ins, dma=True)
                for j in range(32):
                    if j >= 2:
                        S.wk(g, "dve", ("p0w", j - 2))
                    S.mark("WL%d" % (j % 2), ("wld", j), em and g.dma_start(
                        p0f[:, (j % 2) * 1024:(j % 2) * 1024 + 1024],
                        wT_in[j // 8, j % 8]), dma=True)
                for i in range(6):
                    if i >= 1:
                        S.wk(g, "STST", ("stst", i - 1))
                        S.wk(g, "LBST", ("p0st", i - 1))
                    S.mark("SLD", ("sld", i), em and g.dma_start(p0s[0:8, :], sT_in[i]),
                           dma=True)
                    S.mark("LLD", ("lld", i), em and g.dma_start(
                        p0l[0:33, 0:D], lbTa_in[i]), dma=True)

            def p0_dve(v):
                em = S.em
                S.wk(v, "C0", ("consts",))
                if em:
                    v.memset(onesb[:], 1.0)
                    v.tensor_copy(bvrow[:], p0f[0:1, 3584:3584 + D])
                S.mark("dve", ("latc",), em and v.tensor_copy(
                    p0c[:], p0f[:, 2048:2048 + 1536]))
                for j in range(32):
                    S.wk(v, "WL%d" % (j % 2), ("wld", j))
                    S.mark("dve", ("p0w", j), em and v.tensor_copy(
                        p0b[:, (j % 2) * 1024:(j % 2) * 1024 + 1024],
                        p0f[:, (j % 2) * 1024:(j % 2) * 1024 + 1024]))
                for i in range(6):
                    S.wk(v, "SLD", ("sld", i))
                    S.mark("dve", ("p0s", i), em and v.tensor_copy(
                        p0sb[0:8, :], p0s[0:8, :]))
                    S.wk(v, "LLD", ("lld", i))
                    S.mark("dve", ("p0l", i), em and v.tensor_copy(
                        p0lb[0:33, 0:D], p0l[0:33, 0:D]))

            def p0_sy(sy):
                em = S.em
                S.wk(sy, "dve", ("latc",))
                S.bump("WST", em and sy.dma_start(laT_bf, p0c[:]), dma=True)
                for j in range(32):
                    S.wk(sy, "dve", ("p0w", j))
                    S.bump("WST", em and sy.dma_start(
                        w_bf[j // 8, j % 8],
                        p0b[:, (j % 2) * 1024:(j % 2) * 1024 + 1024]), dma=True)
                for i in range(6):
                    S.wk(sy, "dve", ("p0s", i))
                    S.mark("STST", ("stst", i), em and sy.dma_start(
                        sT_bf[i], p0sb[0:8, :]), dma=True)
                    S.wk(sy, "dve", ("p0l", i))
                    S.mark("LBST", ("p0st", i), em and sy.dma_start(
                        lbTa_bf[i], p0lb[0:33, 0:D]), dma=True)
                for nm in ("WST", "STST", "LBST", "SLD", "LLD", "C0", "WL0", "WL1"):
                    S.wt(sy, nm)

            two_pass(b0, [("gpsimd", p0_gp), ("vector", p0_dve), ("sync", p0_sy)])

        _eA = ExitStack()
        with _eA:
            qrot = _eA.enter_context(nc.sbuf_tensor("qrot", [128, NCH * OT], BF16))
            krot = _eA.enter_context(nc.sbuf_tensor("krot", [128, NCH * T], BF16))
            vtm = _eA.enter_context(nc.sbuf_tensor("vtm", [128, NKT * D], BF16))
            _e1 = ExitStack()
            with _e1:
                Xb = _e1.enter_context(nc.sbuf_tensor("Xb", [128, 4096], BF16))
                wstr = _e1.enter_context(nc.sbuf_tensor("wstr", [128, 2048], BF16))
                xtile = _e1.enter_context(nc.sbuf_tensor("xtile", [128, 2048], F32))
                xfm = _e1.enter_context(nc.sbuf_tensor("xfm", [128, 1024], F32))
                x2t = _e1.enter_context(nc.sbuf_tensor("x2t", [128, 1024], BF16))
                swb = _e1.enter_context(nc.sbuf_tensor("swb", [128, 1024], BF16))
                rpb = _e1.enter_context(nc.sbuf_tensor("rpb", [128, 2048], BF16))
                b1 = _e1.enter_context(nc.Block())
                _worder = []
                for tb2 in range(NTB):
                    for d2 in range(NCH):
                        _worder.append(("kmm", tb2, d2))
                    for d2 in range(NCH):
                        _worder.append(("vmm", tb2, d2))
                    if tb2 < 4:
                        for d2 in range(NCH):
                            _worder.append(("qmm", tb2, d2))
                _wpos = {k: i for i, k in enumerate(_worder)}

                def prev_wslot(key):
                    i = _wpos[key]
                    return _worder[i - 2] if i >= 2 else None

                def p1_pe(t):
                    em = S.em
                    for tb in range(NTB):
                        own = tb < 4
                        prev_own = (tb - 1) < 4
                        for tt in range(4):
                            g = tb * 4 + tt
                            S.wk(t, "XT%d" % (g % 2), ("xload", g))
                            for ch in range(NCH):
                                if tb > 0 and tt == 0:
                                    if prev_own:
                                        S.wk(t, "dve", ("eq", tb - 1, ch))
                                    else:
                                        S.wk(t, "dve", ("ev", tb - 1, ch))
                                ins = em and t.transpose(
                                    xbank(ch)[:, tt * 128:(tt + 1) * 128],
                                    xtile[:, (g % 2) * 1024 + ch * 128:
                                          (g % 2) * 1024 + (ch + 1) * 128],
                                    identf[:])
                                if ch == NCH - 1:
                                    S.mark("pe", ("tp", tb, tt), ins)
                        for ch in range(NCH):
                            S.wk(t, "dve", ("sq", tb, ch))
                            em and t.matmul(
                                bankA(0)[0:1, :], ones_col,
                                Xb[:, ch * 512:(ch + 1) * 512],
                                start=(ch == 0), stop=(ch == NCH - 1),
                                tile_position=(0, 0))
                            i2 = em and t.matmul(
                                bankA(0)[32:33, :], ones_col,
                                x2t[:, (ch % 2) * 512:(ch % 2) * 512 + 512],
                                start=(ch == 0), stop=(ch == NCH - 1),
                                tile_position=(0, 32))
                            if ch == NCH - 1:
                                S.mark("pe", ("st", tb), i2)
                            else:
                                S.mark("pe", ("stc", tb, ch), i2)
                        S.wk(t, "dve", ("rows2", tb))
                        em and t.matmul(ps_a[:, 512:1024], ones_r128,
                                        rowsb[0:1, 0:512], start=True, stop=True)
                        S.mark("pe", ("ab", tb), em and t.matmul(
                            ps_a[:, 1024:1536], ones_r128,
                            rowsb[0:1, 512:1024], start=True, stop=True))
                        S.wk(t, "dve", ("nrm", tb))
                        for d in range(NCH):
                            S.wk(t, "WS%d" % (d % 2), ("wk", tb, d))
                            for o in range(NCH):
                                ins = em and t.matmul(
                                    xbank(o),
                                    wstr[:, (d % 2) * 1024 + o * 128:
                                         (d % 2) * 1024 + (o + 1) * 128],
                                    Xb[:, d * 512:(d + 1) * 512],
                                    start=(d == 0), stop=(d == NCH - 1))
                            S.mark("pe", ("kmm", tb, d), ins)
                        for d in range(NCH):
                            S.wk(t, "WS%d" % (d % 2), ("wv", tb, d))
                            if d == 0:
                                for o in range(NCH):
                                    S.wk(t, "dve", ("ek", tb, o))
                            for j in range(8):
                                tt, ob = j // 2, j % 2
                                ins = em and t.matmul(
                                    xbank(j),
                                    Xb[:, d * 512 + tt * 128:d * 512 + (tt + 1) * 128],
                                    wstr[:, (d % 2) * 1024 + ob * 512:
                                         (d % 2) * 1024 + (ob + 1) * 512],
                                    start=(d == 0), stop=False)
                            S.mark("pe", ("vmm", tb, d), ins)
                        for j in range(8):
                            tt, ob = j // 2, j % 2
                            ins = em and t.matmul(
                                xbank(j), ones_r128,
                                bvrow[0:1, ob * 512:(ob + 1) * 512],
                                start=False, stop=True)
                            if j == 7:
                                S.mark("pe", ("vb", tb), ins)
                        if own:
                            for d in range(NCH):
                                S.wk(t, "WS%d" % (d % 2), ("wq", tb, d))
                                if d == 0:
                                    for j in range(8):
                                        S.wk(t, "dve", ("ev", tb, j))
                                for o in range(NCH):
                                    ins = em and t.matmul(
                                        xbank(o),
                                        wstr[:, (d % 2) * 1024 + o * 128:
                                             (d % 2) * 1024 + (o + 1) * 128],
                                        Xb[:, d * 512:(d + 1) * 512],
                                        start=(d == 0), stop=(d == NCH - 1))
                                S.mark("pe", ("qmm", tb, d), ins)
                    for which, n_sl in (("q", 4), ("k", 8)):
                        for ch in range(NCH):
                            S.wk(t, "dve", ("bsq", which, ch))
                            for s in range(n_sl):
                                if s > 0:
                                    S.wk(t, "dve", ("bred", which, ch, s - 1))
                                elif ch > 0:
                                    S.wk(t, "dve", ("bred", which, ch - 1,
                                                    (4 if which == "q" else 8) - 1))
                                elif which == "k":
                                    S.wk(t, "dve", ("bred", "q", NCH - 1, 3))
                                em and t.matmul(
                                    bankB(3)[0:1, :], onesb[0:64, 0:1],
                                    Xb[0:64, s * 512:(s + 1) * 512],
                                    start=True, stop=True, tile_position=(0, 0))
                                S.mark("pe", ("bmm", which, ch, s), em and t.matmul(
                                    bankB(3)[32:33, :], onesb[64:128, 0:1],
                                    Xb[64:128, s * 512:(s + 1) * 512],
                                    start=True, stop=True, tile_position=(64, 32)))
                    S.wk(t, "dve", ("bfin",))
                    S.mark("pe", ("brep",), em and t.matmul(
                        ps_b[:, 0:8], ones_r128, rowsb[0:1, 0:8],
                        start=True, stop=True))

                def p1_act(s):
                    em = S.em
                    for tb in range(NTB):
                        own = tb < 4
                        S.wk(s, "dve", ("rows1", tb))
                        em and s.activation(rows[0:1, 512:1024], rows[0:1, 0:512],
                                            AF.Ln)
                        S.mark("act", ("rows", tb), em and s.activation(
                            rows2[0:1, 512:1024], rows[0:1, 512:1024],
                            AF.Exp, scale=-0.5))
                    S.wk(s, "dve", ("bprod",))
                    for r0 in (0, 32):
                        em and s.activation(rows[r0:r0 + 1, 0:8],
                                            rows2[r0:r0 + 1, 0:8], AF.Ln)
                        S.mark("act", ("bexp", r0), em and s.activation(
                            rows2[r0:r0 + 1, 8:16], rows[r0:r0 + 1, 0:8],
                            AF.Exp, scale=0.5))

                def p1_dve(v):
                    em = S.em
                    gx = 0
                    for tb in range(NTB):
                        own = tb < 4
                        for ch in range(NCH):
                            S.wk(v, "pe", ("tp", tb, 3))
                            S.mark("dve", ("xcast", tb, ch), em and v.tensor_copy(
                                Xb[:, ch * 512:(ch + 1) * 512], xbank(ch)))
                            if own:
                                if gx >= 2:
                                    S.wk(v, "XS%d" % (gx % 2),
                                         ("xscr", (gx - 2) // 8, (gx - 2) % 8))
                                S.mark("dve", ("xfmc", tb, ch), em and v.tensor_copy(
                                    xfm[:, (ch % 2) * 512:(ch % 2) * 512 + 512],
                                    xbank(ch)))
                                gx += 1
                            if ch >= 2:
                                S.wk(v, "pe", ("stc", tb, ch - 2))
                            elif tb > 0:
                                S.wk(v, "pe", ("st", tb - 1))
                            S.mark("dve", ("sq", tb, ch), em and v.tensor_tensor(
                                x2t[:, (ch % 2) * 512:(ch % 2) * 512 + 512],
                                Xb[:, ch * 512:(ch + 1) * 512],
                                Xb[:, ch * 512:(ch + 1) * 512], ALU.mult))
                        S.wk(v, "pe", ("st", tb))
                        if tb > 0:
                            S.wk(v, "M2", ("m2mv", tb - 1))
                        if em:
                            v.tensor_scalar(out=rows2[0:1, 0:512],
                                            in0=bankA(0)[0:1, :],
                                            scalar1=1.0 / D, scalar2=None,
                                            op0=ALU.mult)
                            v.tensor_scalar(out=rows2[32:33, 0:512],
                                            in0=bankA(0)[32:33, :],
                                            scalar1=1.0 / D, scalar2=EPS,
                                            op0=ALU.mult, op1=ALU.add)
                        S.mark("dve", ("rmu", tb), em and v.tensor_mul(
                            rows2[0:1, 512:1024], rows2[0:1, 0:512],
                            rows2[0:1, 0:512]))
                        S.wk(v, "M2", ("m2mv", tb))
                        S.mark("dve", ("rows1", tb), em and v.tensor_sub(
                            rows[0:1, 0:512], rows[0:1, 512:1024],
                            rows2[0:1, 512:1024]))
                        S.wk(v, "act", ("rows", tb))
                        em and v.tensor_copy(rowsb[0:1, 0:512], rows2[0:1, 512:1024])
                        S.mark("dve", ("rows2", tb), em and v.scalar_tensor_tensor(
                            out=rowsb[0:1, 512:1024], in0=rows2[0:1, 0:512],
                            scalar=-1.0, in1=rows2[0:1, 512:1024],
                            op0=ALU.mult, op1=ALU.mult))
                        S.wk(v, "pe", ("ab", tb))
                        for ch in range(NCH):
                            em and v.tensor_tensor(
                                Xb[:, ch * 512:(ch + 1) * 512],
                                Xb[:, ch * 512:(ch + 1) * 512],
                                ps_a[:, 512:1024], ALU.mult)
                            ins = em and v.tensor_tensor(
                                Xb[:, ch * 512:(ch + 1) * 512],
                                Xb[:, ch * 512:(ch + 1) * 512],
                                ps_a[:, 1024:1536], ALU.add)
                            if ch == NCH - 1:
                                S.mark("dve", ("nrm", tb), ins)
                        S.wk(v, "pe", ("kmm", tb, NCH - 1))
                        for o in range(NCH):
                            S.mark("dve", ("ek", tb, o), em and v.tensor_scalar(
                                out=krot[:, o * T + tb * 512:o * T + tb * 512 + 512],
                                in0=xbank(o), scalar1=col_bk(o), scalar2=None,
                                op0=ALU.add))
                        S.wk(v, "pe", ("vb", tb))
                        for j in range(8):
                            tt, ob = j // 2, j % 2
                            ti = tb * 4 + tt
                            S.mark("dve", ("ev", tb, j), em and v.tensor_copy(
                                vtm[:, ti * D + ob * 512:ti * D + ob * 512 + 512],
                                xbank(j)))
                        if own:
                            S.wk(v, "pe", ("qmm", tb, NCH - 1))
                            for o in range(NCH):
                                S.mark("dve", ("eq", tb, o), em and v.tensor_scalar(
                                    out=qrot[:, o * OT + tb * 512:
                                             o * OT + tb * 512 + 512],
                                    in0=xbank(o), scalar1=col_bq(o), scalar2=None,
                                    op0=ALU.add))
                        for o in range(NCH):
                            S.wk(v, "SW%d" % (o % 2), ("swk", tb, o))
                            S.wk(v, "RT%d" % (tb % 2), ("rpt", tb))
                            sl = krot[:, o * T + tb * 512:o * T + tb * 512 + 512]
                            par = o % 2
                            if em:
                                v.tensor_tensor(swb[:, par * 512:par * 512 + 512],
                                                swb[:, par * 512:par * 512 + 512],
                                                rpb[:, (tb % 2) * 1024 + 512:
                                                    (tb % 2) * 1024 + 1024], ALU.mult)
                                v.tensor_tensor(sl, sl,
                                                rpb[:, (tb % 2) * 1024:
                                                    (tb % 2) * 1024 + 512], ALU.mult)
                            S.mark("dve", ("rpk", tb, o), em and v.tensor_tensor(
                                sl, sl, swb[:, par * 512:par * 512 + 512], ALU.add))
                        if own:
                            for o in range(NCH):
                                S.wk(v, "SW%d" % (o % 2), ("swq", tb, o))
                                sl = qrot[:, o * OT + tb * 512:
                                          o * OT + tb * 512 + 512]
                                par = o % 2
                                if em:
                                    v.tensor_tensor(swb[:, par * 512:par * 512 + 512],
                                                    swb[:, par * 512:par * 512 + 512],
                                                    rpb[:, (tb % 2) * 1024 + 512:
                                                        (tb % 2) * 1024 + 1024],
                                                    ALU.mult)
                                    v.tensor_tensor(sl, sl,
                                                    rpb[:, (tb % 2) * 1024:
                                                        (tb % 2) * 1024 + 512],
                                                    ALU.mult)
                                S.mark("dve", ("rpq", tb, o), em and v.tensor_tensor(
                                    sl, sl, swb[:, par * 512:par * 512 + 512],
                                    ALU.add))
                    if em:
                        v.memset(nqmx[:], 0.0)
                        v.memset(nkmx[:], 0.0)
                    for which, n_sl, src, mx, w in (("q", 4, qrot, nqmx, OT),
                                                    ("k", 8, krot, nkmx, T)):
                        for ch in range(NCH):
                            ins = None
                            for s in range(n_sl):
                                ins = em and v.tensor_tensor(
                                    Xb[:, s * 512:(s + 1) * 512],
                                    src[:, ch * w + s * 512:ch * w + (s + 1) * 512],
                                    src[:, ch * w + s * 512:ch * w + (s + 1) * 512],
                                    ALU.mult)
                            S.mark("dve", ("bsq", which, ch), ins)
                            for s in range(n_sl):
                                S.wk(v, "pe", ("bmm", which, ch, s))
                                em and v.reduce_max(rows[0:33, 0:1],
                                                    bankB(3)[0:33, :],
                                                    axis=mybir.AxisListType.X)
                                S.mark("dve", ("bred", which, ch, s),
                                       em and v.tensor_max(mx[0:33, ch:ch + 1],
                                                           mx[0:33, ch:ch + 1],
                                                           rows[0:33, 0:1]))
                    em and v.tensor_mul(rows2[0:33, 0:8], nqmx[0:33, 0:8],
                                        nkmx[0:33, 0:8])
                    S.mark("dve", ("bprod",), em and v.tensor_scalar(
                        out=rows2[0:33, 0:8], in0=rows2[0:33, 0:8],
                        scalar1=SAFETY, scalar2=None, op0=ALU.mult))
                    S.wk(v, "act", ("bexp", 32))
                    S.wk(v, "BM", ("bmv",))
                    if em:
                        v.tensor_max(rows2[0:1, 16:24], rows2[0:1, 8:16],
                                     rows[0:1, 8:16])
                        v.tensor_scalar(out=rows2[0:1, 16:24],
                                        in0=rows2[0:1, 16:24],
                                        scalar1=-1.0, scalar2=None, op0=ALU.mult)
                    S.mark("dve", ("bfin",), em and v.tensor_copy(
                        rowsb[0:1, 0:8], rows2[0:1, 16:24]))
                    S.wk(v, "pe", ("brep",))
                    S.mark("dve", ("bias",), em and v.tensor_copy(
                        biasM[:], ps_b[:, 0:8]))

                def p1_sy(sy):
                    em = S.em
                    for tb in range(NTB):
                        own = tb < 4
                        if tb >= 2:
                            S.wk(sy, "dve", ("rpq" if (tb - 2) < 4 else "rpk",
                                             tb - 2, NCH - 1))
                        S.bump("RT%d" % (tb % 2), em and sy.dma_start(
                            rpb[:, (tb % 2) * 1024:(tb % 2) * 1024 + 512],
                            ropeC[:, tb * 512:(tb + 1) * 512]), dma=True)
                        S.mark("RT%d" % (tb % 2), ("rpt", tb), em and sy.dma_start(
                            rpb[:, (tb % 2) * 1024 + 512:(tb % 2) * 1024 + 1024],
                            ropeS[:, tb * 512:(tb + 1) * 512]), dma=True)
                        for tt in range(4):
                            g = tb * 4 + tt
                            if g >= 2:
                                S.wk(sy, "pe", ("tp", (g - 2) // 4, (g - 2) % 4))
                            S.mark("XT%d" % (g % 2), ("xload", g), em and sy.dma_start(
                                xtile[:, (g % 2) * 1024:(g % 2) * 1024 + 1024],
                                x_in[g * 128:(g + 1) * 128, :]), dma=True)
                        if own:
                            for ch in range(NCH):
                                S.wk(sy, "dve", ("xfmc", tb, ch))
                                S.mark("XS%d" % (ch % 2), ("xscr", tb, ch),
                                       em and sy.dma_start(
                                    xscr[ch, :, tb * 512:(tb + 1) * 512],
                                    xfm[:, (ch % 2) * 512:(ch % 2) * 512 + 512]),
                                    dma=True)
                        S.wk(sy, "dve", ("rmu", tb))
                        S.mark("M2", ("m2mv", tb), em and sy.dma_start(
                            rows[0:1, 512:1024], rows2[32:33, 0:512]), dma=True)
                        if debug:
                            S.wk(sy, "dve", ("nrm", tb))
                            S.bump("sy", em and sy.dma_start(dbg_ln1[tb], Xb[:]),
                                   dma=True)
                    S.wk(sy, "act", ("bexp", 32))
                    S.mark("BM", ("bmv",), em and sy.dma_start(
                        rows[0:1, 8:16], rows2[32:33, 8:16]), dma=True)
                    if debug:
                        S.wk(sy, "dve", ("bias",))
                        S.bump("sy", em and sy.dma_start(dbg_q, qrot[:]), dma=True)
                        S.bump("sy", em and sy.dma_start(dbg_k, krot[:]), dma=True)
                        S.bump("sy", em and sy.dma_start(dbg_v, vtm[:]), dma=True)
                        S.bump("sy", em and sy.dma_start(dbg_b, biasM[:]), dma=True)
                    if em:
                        sy.wait_ge(sy_q, S.n["sy"])
                    for nm in ("XS0", "XS1", "M2", "BM"):
                        S.wt(sy, nm)

                def p1_gp(g):
                    em = S.em
                    for tb in range(NTB):
                        own = tb < 4
                        for kind, widx in (("wk", 1), ("wv", 2)) + (
                                (("wq", 0),) if own else ()):
                            mmk = {"wk": "kmm", "wv": "vmm", "wq": "qmm"}[kind]
                            for d in range(NCH):
                                prev = prev_wslot((mmk, tb, d))
                                if prev is not None:
                                    S.wk(g, "pe", prev)
                                S.mark("WS%d" % (d % 2), (kind, tb, d),
                                       em and g.dma_start(
                                    wstr[:, (d % 2) * 1024:(d % 2) * 1024 + 1024],
                                    w_bf[widx, d]), dma=True)
                        for o in range(NCH):
                            S.wk(g, "dve", ("ek", tb, o))
                            if o >= 2:
                                S.wk(g, "dve", ("rpk", tb, o - 2))
                            elif tb > 0:
                                S.wk(g, "dve", ("rpq" if (tb - 1) < 4 else "rpk",
                                                tb - 1, NCH - 2 + o))
                            sl0 = o * T + tb * 512
                            par = o % 2
                            for (a, bb) in ((0, 32), (32, 0), (64, 96)):
                                S.bump("SW%d" % par, em and g.dma_start(
                                    swb[a:a + 32, par * 512:par * 512 + 512],
                                    krot[bb:bb + 32, sl0:sl0 + 512]), dma=True)
                            S.mark("SW%d" % par, ("swk", tb, o), em and g.dma_start(
                                swb[96:128, par * 512:par * 512 + 512],
                                krot[64:96, sl0:sl0 + 512]), dma=True)
                        if own:
                            for o in range(NCH):
                                S.wk(g, "dve", ("eq", tb, o))
                                S.wk(g, "dve", ("rpk", tb, NCH - 2 + o)
                                     if o < 2 else ("rpq", tb, o - 2))
                                sl0 = o * OT + tb * 512
                                par = o % 2
                                for (a, bb) in ((0, 32), (32, 0), (64, 96)):
                                    S.bump("SW%d" % par, em and g.dma_start(
                                        swb[a:a + 32, par * 512:par * 512 + 512],
                                        qrot[bb:bb + 32, sl0:sl0 + 512]), dma=True)
                                S.mark("SW%d" % par, ("swq", tb, o), em and g.dma_start(
                                    swb[96:128, par * 512:par * 512 + 512],
                                    qrot[64:96, sl0:sl0 + 512]), dma=True)

                two_pass(b1, [("tensor", p1_pe), ("scalar", p1_act),
                              ("vector", p1_dve), ("sync", p1_sy),
                              ("gpsimd", p1_gp)])

            _e2 = ExitStack()
            with _e2:
                pp = _e2.enter_context(nc.sbuf_tensor("pp", [128, 2048], BF16))
                otile = _e2.enter_context(nc.sbuf_tensor("otile", [128, 1024], BF16))
                repb = _e2.enter_context(nc.sbuf_tensor("repb", [128, 512], F32))
                b2 = _e2.enter_context(nc.Block())
                pairs = [(qb, p) for qb in range(NQB) for p in range(NCH)]

                def gof(qb, p, kt):
                    return (qb * 8 + p) * 32 + kt

                def unp(gi):
                    kt = gi % 32
                    p = (gi // 32) % 8
                    qb = gi // 256
                    return (qb, p, kt)

                def p2_pe(t):
                    em = S.em
                    for pi, (qb, p) in enumerate(pairs):
                        opar = p % 2
                        obank = bankB(opar)
                        dbank = bankB(2 + opar)
                        for kt in range(NKT):
                            gi = gof(qb, p, kt)
                            if gi >= 2:
                                S.wk(t, "act", ("xp",) + unp(gi - 2))
                            spar = kt % 2
                            em and t.matmul(
                                ps_a[:, spar * 1024:spar * 1024 + 512],
                                krot[0:64, p * T + kt * 128:p * T + (kt + 1) * 128],
                                qrot[0:64, p * OT + qb * 512:p * OT + qb * 512 + 512],
                                start=True, stop=True, tile_position=(0, 0))
                            S.mark("pe", ("s", qb, p, kt), em and t.matmul(
                                ps_a[:, spar * 1024 + 512:spar * 1024 + 1024],
                                krot[64:128, p * T + kt * 128:p * T + (kt + 1) * 128],
                                qrot[64:128, p * OT + qb * 512:p * OT + qb * 512 + 512],
                                start=True, stop=True, tile_position=(64, 0)))
                            if kt == 1 and pi > 0:
                                pq, pp_ = pairs[pi - 1]
                                ppar = pp_ % 2
                                S.wk(t, "dve", ("rcp", pq, pp_))
                                em and t.matmul(
                                    ps_b[0:64, (2 + ppar) * 512:(2 + ppar) * 512 + 512],
                                    ones_r64, rowsb[0:1, 0:512],
                                    start=True, stop=True, tile_position=(0, 0))
                                S.mark("pe", ("rep", pq, pp_), em and t.matmul(
                                    ps_b[64:128, (2 + ppar) * 512:
                                         (2 + ppar) * 512 + 512],
                                    ones_r64, rowsb[0:1, 512:1024],
                                    start=True, stop=True, tile_position=(0, 64)))
                            if kt >= 1:
                                kv = kt - 1
                                S.wk(t, "act", ("xp", qb, p, kv))
                                if kv == 0 and pi >= 2:
                                    S.wk(t, "dve", ("oev",) + tuple(pairs[pi - 2]))
                                par1 = kv % 2
                                em and t.matmul(
                                    obank[0:64, :] if False else
                                    ps_b[0:64, opar * 512:opar * 512 + 512],
                                    vtm[:, kv * D + (2 * p) * 64:
                                        kv * D + (2 * p) * 64 + 64],
                                    pp[:, par1 * 1024:par1 * 1024 + 512],
                                    start=(kv == 0), stop=(kv == NKT - 1),
                                    tile_position=(0, 0))
                                em and t.matmul(
                                    ps_b[64:128, opar * 512:opar * 512 + 512],
                                    vtm[:, kv * D + (2 * p + 1) * 64:
                                        kv * D + (2 * p + 1) * 64 + 64],
                                    pp[:, par1 * 1024 + 512:par1 * 1024 + 1024],
                                    start=(kv == 0), stop=(kv == NKT - 1),
                                    tile_position=(0, 64))
                                em and t.matmul(
                                    ps_b[0:1, (2 + opar) * 512:(2 + opar) * 512 + 512],
                                    ones_col, pp[:, par1 * 1024:par1 * 1024 + 512],
                                    start=(kv == 0), stop=(kv == NKT - 1),
                                    tile_position=(0, 0))
                                S.mark("pe", ("pv", qb, p, kv), em and t.matmul(
                                    ps_b[32:33, (2 + opar) * 512:
                                         (2 + opar) * 512 + 512],
                                    ones_col,
                                    pp[:, par1 * 1024 + 512:par1 * 1024 + 1024],
                                    start=(kv == 0), stop=(kv == NKT - 1),
                                    tile_position=(0, 32)))
                        kv = NKT - 1
                        S.wk(t, "act", ("xp", qb, p, kv))
                        par1 = kv % 2
                        em and t.matmul(
                            ps_b[0:64, opar * 512:opar * 512 + 512],
                            vtm[:, kv * D + (2 * p) * 64:kv * D + (2 * p) * 64 + 64],
                            pp[:, par1 * 1024:par1 * 1024 + 512],
                            start=False, stop=True, tile_position=(0, 0))
                        em and t.matmul(
                            ps_b[64:128, opar * 512:opar * 512 + 512],
                            vtm[:, kv * D + (2 * p + 1) * 64:
                                kv * D + (2 * p + 1) * 64 + 64],
                            pp[:, par1 * 1024 + 512:par1 * 1024 + 1024],
                            start=False, stop=True, tile_position=(0, 64))
                        em and t.matmul(
                            ps_b[0:1, (2 + opar) * 512:(2 + opar) * 512 + 512],
                            ones_col, pp[:, par1 * 1024:par1 * 1024 + 512],
                            start=False, stop=True, tile_position=(0, 0))
                        S.mark("pe", ("pv", qb, p, kv), em and t.matmul(
                            ps_b[32:33, (2 + opar) * 512:(2 + opar) * 512 + 512],
                            ones_col, pp[:, par1 * 1024 + 512:par1 * 1024 + 1024],
                            start=False, stop=True, tile_position=(0, 32)))
                    pq, pp_ = pairs[-1]
                    ppar = pp_ % 2
                    S.wk(t, "dve", ("rcp", pq, pp_))
                    em and t.matmul(
                        ps_b[0:64, (2 + ppar) * 512:(2 + ppar) * 512 + 512],
                        ones_r64, rowsb[0:1, 0:512], start=True, stop=True,
                        tile_position=(0, 0))
                    S.mark("pe", ("rep", pq, pp_), em and t.matmul(
                        ps_b[64:128, (2 + ppar) * 512:(2 + ppar) * 512 + 512],
                        ones_r64, rowsb[0:1, 512:1024], start=True, stop=True,
                        tile_position=(0, 64)))

                def p2_act(s):
                    em = S.em
                    for (qb, p) in pairs:
                        for kt in range(NKT):
                            gi = gof(qb, p, kt)
                            S.wk(s, "pe", ("s", qb, p, kt))
                            if gi >= 2:
                                S.wk(s, "pe", ("pv",) + unp(gi - 2))
                            spar = kt % 2
                            S.mark("act", ("xp", qb, p, kt), em and s.activation(
                                pp[:, spar * 1024:spar * 1024 + 1024],
                                ps_a[:, spar * 1024:spar * 1024 + 1024],
                                AF.Exp, bias=biasM[:, p:p + 1], scale=0.125))

                def p2_dve(v):
                    em = S.em
                    for pi, (qb, p) in enumerate(pairs):
                        opar = p % 2
                        S.wk(v, "pe", ("pv", qb, p, NKT - 1))
                        em and v.tensor_copy(rows2[0:1, 0:512],
                                             ps_b[0:1, (2 + opar) * 512:
                                                  (2 + opar) * 512 + 512])
                        S.mark("dve", ("dcp", qb, p), em and v.tensor_copy(
                            rows2[32:33, 0:512],
                            ps_b[32:33, (2 + opar) * 512:(2 + opar) * 512 + 512]))
                        S.wk(v, "DM", ("dmv", qb, p))
                        em and v.reciprocal(rows[0:1, 0:1024], rows2[0:1, 0:1024])
                        S.mark("dve", ("rcp", qb, p), em and v.tensor_copy(
                            rowsb[0:1, 0:1024], rows[0:1, 0:1024]))
                        S.wk(v, "pe", ("rep", qb, p))
                        em and v.tensor_copy(repb[:],
                                             ps_b[:, (2 + opar) * 512:
                                                  (2 + opar) * 512 + 512])
                        if pi >= 2:
                            S.wk(v, "OS%d" % (p % 2), ("ost",) + tuple(pairs[pi - 2]))
                        S.mark("dve", ("oev", qb, p), em and v.tensor_tensor(
                            otile[:, opar * 512:opar * 512 + 512],
                            ps_b[:, opar * 512:opar * 512 + 512],
                            repb[:], ALU.mult))

                def p2_sy(sy):
                    em = S.em
                    for (qb, p) in pairs:
                        S.wk(sy, "dve", ("dcp", qb, p))
                        S.mark("DM", ("dmv", qb, p), em and sy.dma_start(
                            rows2[0:1, 512:1024], rows2[32:33, 0:512]), dma=True)
                        S.wk(sy, "dve", ("oev", qb, p))
                        S.mark("OS%d" % (p % 2), ("ost", qb, p), em and sy.dma_start(
                            oscr[p, :, qb * 512:(qb + 1) * 512],
                            otile[:, (p % 2) * 512:(p % 2) * 512 + 512]), dma=True)
                    for nm in ("OS0", "OS1", "DM"):
                        S.wt(sy, nm)

                two_pass(b2, [("tensor", p2_pe), ("scalar", p2_act),
                              ("vector", p2_dve), ("sync", p2_sy)])

        _e3 = ExitStack()
        with _e3:
            woTb = _e3.enter_context(nc.sbuf_tensor("woTb", [128, 8192], BF16))
            obuf = _e3.enter_context(nc.sbuf_tensor("obuf", [128, 4096], BF16))
            xfmb = _e3.enter_context(nc.sbuf_tensor("xfmb", [128, 1024], F32))
            ptile = _e3.enter_context(nc.sbuf_tensor("ptile", [128, 1024], F32))
            x2f = _e3.enter_context(nc.sbuf_tensor("x2f", [128, 1024], F32))
            X2 = _e3.enter_context(nc.sbuf_tensor("X2", [128, 4096], BF16))
            x2sq = _e3.enter_context(nc.sbuf_tensor("x2sq", [128, 4096], BF16))
            l2f = _e3.enter_context(nc.sbuf_tensor("l2f", [128, 1024], F32))
            b3 = _e3.enter_context(nc.Block())
            def p3_pe(t):
                em = S.em
                for tb in range(NQB):
                    S.wk(t, "OB", ("ob", tb))
                    if tb == 0:
                        S.wk(t, "WOT", ("wo",))
                    for o in range(NCH):
                        if o >= 4:
                            S.wk(t, "dve", ("po", tb, o - 4))
                        elif tb > 0:
                            S.wk(t, "dve", ("po", tb - 1, 4 + o))
                        for f in range(NCH):
                            ins = em and t.matmul(
                                bankA(o % 4),
                                woTb[:, f * 1024 + o * 128:f * 1024 + (o + 1) * 128],
                                obuf[:, f * 512:(f + 1) * 512],
                                start=(f == 0), stop=(f == NCH - 1))
                        S.mark("pe", ("pmm", tb, o), ins)
                    for o in range(NCH):
                        S.wk(t, "dve", ("sq3", tb, o))
                        if o == 0 and tb > 0:
                            S.wk(t, "dve", ("rw1", tb - 1))
                        em and t.matmul(bankB(0)[0:1, :], ones_col,
                                        X2[:, o * 512:(o + 1) * 512],
                                        start=(o == 0), stop=(o == NCH - 1),
                                        tile_position=(0, 0))
                        ins = em and t.matmul(bankB(0)[32:33, :], ones_col,
                                              x2sq[:, o * 512:(o + 1) * 512],
                                              start=(o == 0), stop=(o == NCH - 1),
                                              tile_position=(0, 32))
                    S.mark("pe", ("st3", tb), ins)
                    S.wk(t, "dve", ("rw2", tb))
                    if tb > 0:
                        S.wk(t, "dve", ("l2", tb - 1, NCH - 1))
                    em and t.matmul(bankB(1), ones_r128, rowsb[0:1, 0:512],
                                    start=True, stop=True)
                    S.mark("pe", ("ab3", tb), em and t.matmul(
                        bankB(2), ones_r128, rowsb[0:1, 512:1024],
                        start=True, stop=True))

            def p3_act(s):
                em = S.em
                for tb in range(NQB):
                    S.wk(s, "dve", ("rw1", tb))
                    em and s.activation(rows[0:1, 512:1024], rows[0:1, 0:512], AF.Ln)
                    S.mark("act", ("rows3", tb), em and s.activation(
                        rows2[0:1, 512:1024], rows[0:1, 512:1024],
                        AF.Exp, scale=-0.5))

            def p3_dve(v):
                em = S.em
                for tb in range(NQB):
                    for o in range(NCH):
                        S.wk(v, "pe", ("pmm", tb, o))
                        S.wk(v, "XL%d" % (o % 2), ("xfl", tb, o))
                        S.mark("dve", ("po", tb, o), em and v.tensor_scalar(
                            out=ptile[:, (o % 2) * 512:(o % 2) * 512 + 512],
                            in0=bankA(o % 4), scalar1=col_bo(o), scalar2=None,
                            op0=ALU.add))
                        if o >= 2 or tb > 0:
                            po = (tb, o - 2) if o >= 2 else (tb - 1, NCH - 2 + o)
                            S.wk(v, "X2%d" % (o % 2), ("x2st",) + po)
                        par = o % 2
                        S.mark("dve", ("x2", tb, o), em and v.tensor_add(
                            x2f[:, par * 512:par * 512 + 512],
                            ptile[:, par * 512:par * 512 + 512],
                            xfmb[:, par * 512:par * 512 + 512]))
                        em and v.tensor_copy(X2[:, o * 512:(o + 1) * 512],
                                             x2f[:, par * 512:par * 512 + 512])
                        if tb > 0 and o == 0:
                            S.wk(v, "pe", ("st3", tb - 1))
                        S.mark("dve", ("sq3", tb, o), em and v.tensor_tensor(
                            x2sq[:, o * 512:(o + 1) * 512],
                            X2[:, o * 512:(o + 1) * 512],
                            X2[:, o * 512:(o + 1) * 512], ALU.mult))
                    S.wk(v, "pe", ("st3", tb))
                    if tb > 0:
                        S.wk(v, "M2", ("m2v3", tb - 1))
                    if em:
                        v.tensor_scalar(out=rows2[0:1, 0:512], in0=bankB(0)[0:1, :],
                                        scalar1=1.0 / D, scalar2=None, op0=ALU.mult)
                        v.tensor_scalar(out=rows2[32:33, 0:512],
                                        in0=bankB(0)[32:33, :],
                                        scalar1=1.0 / D, scalar2=EPS,
                                        op0=ALU.mult, op1=ALU.add)
                    S.mark("dve", ("rmu3", tb), em and v.tensor_mul(
                        rows2[0:1, 512:1024], rows2[0:1, 0:512], rows2[0:1, 0:512]))
                    S.wk(v, "M2", ("m2v3", tb))
                    S.mark("dve", ("rw1", tb), em and v.tensor_sub(
                        rows[0:1, 0:512], rows[0:1, 512:1024], rows2[0:1, 512:1024]))
                    S.wk(v, "act", ("rows3", tb))
                    em and v.tensor_copy(rowsb[0:1, 0:512], rows2[0:1, 512:1024])
                    S.mark("dve", ("rw2", tb), em and v.scalar_tensor_tensor(
                        out=rowsb[0:1, 512:1024], in0=rows2[0:1, 0:512], scalar=-1.0,
                        in1=rows2[0:1, 512:1024], op0=ALU.mult, op1=ALU.mult))
                    S.wk(v, "pe", ("ab3", tb))
                    for o in range(NCH):
                        if o >= 2 or tb > 0:
                            po = (tb, o - 2) if o >= 2 else (tb - 1, NCH - 2 + o)
                            S.wk(v, "HS%d" % (o % 2), ("hst",) + po)
                        par = o % 2
                        em and v.tensor_tensor(l2f[:, par * 512:par * 512 + 512],
                                               X2[:, o * 512:(o + 1) * 512],
                                               bankB(1), ALU.mult)
                        S.mark("dve", ("l2", tb, o), em and v.tensor_tensor(
                            l2f[:, par * 512:par * 512 + 512],
                            l2f[:, par * 512:par * 512 + 512],
                            bankB(2), ALU.add))

            def p3_sy(sy):
                em = S.em
                for tb in range(NQB):
                    for o in range(NCH):
                        if o >= 2 or tb > 0:
                            po = (tb, o - 2) if o >= 2 else (tb - 1, NCH - 2 + o)
                            S.wk(sy, "dve", ("x2",) + po)
                        S.mark("XL%d" % (o % 2), ("xfl", tb, o), em and sy.dma_start(
                            xfmb[:, (o % 2) * 512:(o % 2) * 512 + 512],
                            xscr[o, :, tb * 512:(tb + 1) * 512]), dma=True)
                        S.wk(sy, "dve", ("x2", tb, o))
                        S.mark("X2%d" % (o % 2), ("x2st", tb, o), em and sy.dma_start(
                            x2scr[o, :, tb * 512:(tb + 1) * 512],
                            x2f[:, (o % 2) * 512:(o % 2) * 512 + 512]), dma=True)
                    S.wk(sy, "dve", ("rmu3", tb))
                    S.mark("M2", ("m2v3", tb), em and sy.dma_start(
                        rows[0:1, 512:1024], rows2[32:33, 0:512]), dma=True)
                    for o in range(NCH):
                        S.wk(sy, "dve", ("l2", tb, o))
                        S.mark("HS%d" % (o % 2), ("hst", tb, o), em and sy.dma_start(
                            hscr[o, :, tb * 512:(tb + 1) * 512],
                            l2f[:, (o % 2) * 512:(o % 2) * 512 + 512]), dma=True)
                for nm in ("X20", "X21", "HS0", "HS1", "M2"):
                    S.wt(sy, nm)

            def p3_gp(g):
                em = S.em
                for f in range(NCH):
                    if f == NCH - 1:
                        S.mark("WOT", ("wo",), em and g.dma_start(
                            woTb[:, f * 1024:(f + 1) * 1024], w_bf[3, f]), dma=True)
                    else:
                        S.bump("WOT", em and g.dma_start(
                            woTb[:, f * 1024:(f + 1) * 1024], w_bf[3, f]), dma=True)
                for tb in range(NQB):
                    if tb > 0:
                        S.wk(g, "pe", ("pmm", tb - 1, NCH - 1))
                    for f in range(NCH):
                        ins = em and g.dma_start(
                            obuf[:, f * 512:(f + 1) * 512],
                            oscr[f, :, tb * 512:(tb + 1) * 512])
                        if f < NCH - 1:
                            S.bump("OB", ins, dma=True)
                        else:
                            S.mark("OB", ("ob", tb), ins, dma=True)

            two_pass(b3, [("tensor", p3_pe), ("scalar", p3_act),
                          ("vector", p3_dve), ("sync", p3_sy), ("gpsimd", p3_gp)])

        _eh = ExitStack()
        with _eh:
          hbuf = _eh.enter_context(nc.sbuf_tensor("hbuf", [128, 16384], F32))
          _e4 = ExitStack()
          with _e4:
            hbf = _e4.enter_context(nc.sbuf_tensor("hbf", [128, 16384], BF16))
            ybuf = _e4.enter_context(nc.sbuf_tensor("ybuf", [128, 16384], BF16))
            wTb = _e4.enter_context(nc.sbuf_tensor("wTb", [128, 16384], BF16))
            qc = _e4.enter_context(nc.sbuf_tensor("qc", [128, 2048], BF16))
            sTb = _e4.enter_context(nc.sbuf_tensor("sTb", [8, 2048], BF16))
            lbb = _e4.enter_context(nc.sbuf_tensor("lbb", [33, 2048], BF16))
            uTa = _e4.enter_context(nc.sbuf_tensor("uTa", [33, 1024], BF16))
            v1t = _e4.enter_context(nc.sbuf_tensor("v1t", [128, 4096], BF16))
            ysq = _e4.enter_context(nc.sbuf_tensor("ysq", [128, 4096], BF16))
            laTb = _e4.enter_context(nc.sbuf_tensor("laTb", [128, 1536], BF16))
            b4 = _e4.enter_context(nc.Block())
            def ibuf_of(i):
                return hbf if i % 2 == 0 else ybuf

            def p4_pe(t):
                em = S.em
                for i in range(6):
                    b = i // 2
                    for ch in range(NCH):
                        S.wk(t, "SB%d" % (ch % 2), ("qld", i, ch))
                        if (i, ch) != (0, 0):
                            pch = (i, ch - 1) if ch > 0 else (i - 1, NCH - 1)
                            S.wk(t, "dve", ("dq",) + pch)
                        if ch == 0 and i % 2 == 1:
                            S.wk(t, "dve", ("nm", i - 1, 1, NCH - 1))
                        par = ch % 2
                        em and t.matmul(bankB(0), emat[0:8, :],
                                        sTb[0:8, par * 1024:par * 1024 + 512],
                                        start=True, stop=True)
                        S.mark("pe", ("se", i, ch), em and t.matmul(
                            bankB(1), emat[0:8, :],
                            sTb[0:8, par * 1024 + 512:par * 1024 + 1024],
                            start=True, stop=True))
                    for tb in range(NQB):
                        if (i, tb) != (0, 0):
                            ptb = (i, tb - 1) if tb > 0 else (i - 1, NQB - 1)
                            S.wk(t, "dve", ("ut",) + ptb)
                        if i % 2 == 1 and tb == 0:
                            S.wk(t, "dve", ("nm", i - 1, 3, NCH - 1))
                        if i % 2 == 0:
                            if b > 0:
                                S.wk(t, "dve", ("hbc", b - 1, NCH - 1))
                            elif tb == 0:
                                S.wk(t, "dve", ("hc0", NCH - 1))
                                S.wk(t, "HLD", ("lat",))
                        else:
                            S.wk(t, "act", ("gel", i - 1, tb, NCH - 1))
                        ins = None
                        for ch in range(NCH):
                            ins = em and t.matmul(
                                bankB(3)[0:32, :],
                                laTb[:, i * 256 + ch * 32:i * 256 + (ch + 1) * 32],
                                ibuf_of(i)[:, ch * 2048 + tb * 512:
                                           ch * 2048 + tb * 512 + 512],
                                start=(ch == 0), stop=(ch == NCH - 1))
                        S.mark("pe", ("umm", i, tb), ins)
                        S.wk(t, "LB%d" % (i % 2), ("lb", i))
                        S.wk(t, "dve", ("ut", i, tb))
                        if tb == 0:
                            S.wk(t, "dve", ("dq", i, NCH - 1))
                        for o in range(NCH):
                            yidx = tb * 8 + o
                            if yidx >= 4:
                                S.wk(t, "dve", ("ye", i, (yidx - 4) // 8,
                                                (yidx - 4) % 8))
                            elif i > 0:
                                S.wk(t, "dve", ("ye", i - 1, (yidx + 28) // 8,
                                                (yidx + 28) % 8))
                            if i > 0 and yidx == 0 and i % 2 == 1:
                                S.wk(t, "dve", ("nm", i - 1, 3, NCH - 1))
                            bank = bankA(yidx % 4)
                            for ch in range(NCH):
                                em and t.matmul(
                                    bank,
                                    wTb[:, (i % 2) * 8192 + ch * 1024 + o * 128:
                                        (i % 2) * 8192 + ch * 1024 + (o + 1) * 128],
                                    ibuf_of(i)[:, ch * 2048 + tb * 512:
                                               ch * 2048 + tb * 512 + 512],
                                    start=(ch == 0), stop=False)
                            S.mark("pe", ("ymm", i, tb, o), em and t.matmul(
                                bank, lbb[0:33, (i % 2) * 1024 + o * 128:
                                          (i % 2) * 1024 + (o + 1) * 128],
                                uTa[0:33, (tb % 2) * 512:(tb % 2) * 512 + 512],
                                start=False, stop=True))
                        if i % 2 == 0:
                            if tb == NQB - 1:
                                S.wk(t, "dve", ("ut", i, tb))
                            for o in range(NCH):
                                S.wk(t, "dve", ("ye", i, tb, o))
                                em and t.matmul(bankB(tb)[0:1, :], ones_col,
                                                ybuf[:, o * 2048 + tb * 512:
                                                     o * 2048 + tb * 512 + 512],
                                                start=(o == 0), stop=(o == NCH - 1),
                                                tile_position=(0, 0))
                                ins = em and t.matmul(
                                    bankB(tb)[32:33, :], ones_col,
                                    ysq[:, o * 512:(o + 1) * 512],
                                    start=(o == 0), stop=(o == NCH - 1),
                                    tile_position=(0, 32))
                            S.mark("pe", ("st4", i, tb), ins)
                    if i % 2 == 0:
                        for tb in range(NQB):
                            S.wk(t, "dve", ("rw24", i, tb))
                            if tb == 0:
                                S.wk(t, "dve", ("ye", i, NQB - 1, NCH - 1))
                            em and t.matmul(bankA(tb), ones_r128,
                                            rowsb[0:1, 0:512], start=True, stop=True)
                            S.mark("pe", ("ab4", i, tb), em and t.matmul(
                                bankB(tb), ones_r128, rowsb[0:1, 512:1024],
                                start=True, stop=True))

            def p4_act(s):
                em = S.em
                for i in range(6):
                    b = i // 2
                    if i % 2 == 0:
                        for tb in range(NQB):
                            S.wk(s, "dve", ("rw14", i, tb))
                            em and s.activation(rows[0:1, 512:1024],
                                                rows[0:1, 0:512], AF.Ln)
                            S.mark("act", ("rows4", i, tb), em and s.activation(
                                rows2[0:1, 512:1024], rows[0:1, 512:1024],
                                AF.Exp, scale=-0.5))
                        for tb in range(NQB):
                            for o in range(NCH):
                                S.wk(s, "dve", ("nm", i, tb, o))
                                S.mark("act", ("gel", i, tb, o), em and s.activation(
                                    ybuf[:, o * 2048 + tb * 512:
                                         o * 2048 + tb * 512 + 512],
                                    v1t[:, o * 512:(o + 1) * 512],
                                    AF.Gelu, scale=col_g(b, o), bias=col_be(b, o)))

            def p4_dve(v):
                em = S.em
                for ch in range(NCH):
                    S.wk(v, "HLD", ("hld",))
                    S.mark("dve", ("hc0", ch), em and v.tensor_copy(
                        hbf[:, ch * 2048:(ch + 1) * 2048],
                        hbuf[:, ch * 2048:(ch + 1) * 2048]))
                if em:
                    v.memset(uTa[32:33, :], 1.0)
                for i in range(6):
                    b = i // 2
                    for ch in range(NCH):
                        S.wk(v, "QB%d" % (ch % 2), ("qbld", i, ch))
                        par = ch % 2
                        S.wk(v, "pe", ("se", i, ch))
                        if i >= 2 or (i >= 1 and i % 2 == 0):
                            pass
                        if i >= 2:
                            S.wk(v, "pe", ("ymm", i - 2, NQB - 1, NCH - 1))
                        S.mark("dve", ("dq", i, ch), em and v.scalar_tensor_tensor(
                            out=wTb[:, (i % 2) * 8192 + ch * 1024:
                                    (i % 2) * 8192 + (ch + 1) * 1024],
                            in0=qc[:, par * 1024:par * 1024 + 1024],
                            scalar=-7.5, in1=ps_b[:, 0:1024],
                            op0=ALU.add, op1=ALU.mult))
                    for tb in range(NQB):
                        S.wk(v, "pe", ("umm", i, tb))
                        S.mark("dve", ("ut", i, tb), em and v.tensor_copy(
                            uTa[0:32, (tb % 2) * 512:(tb % 2) * 512 + 512],
                            bankB(3)[0:32, :]))
                        if i % 2 == 0:
                            for o in range(NCH):
                                S.wk(v, "pe", ("ymm", i, tb, o))
                                em and v.tensor_copy(
                                    ybuf[:, o * 2048 + tb * 512:
                                         o * 2048 + tb * 512 + 512],
                                    bankA((tb * 8 + o) % 4))
                                if o == 0:
                                    if tb > 0:
                                        S.wk(v, "pe", ("st4", i, tb - 1))
                                    elif i >= 2:
                                        S.wk(v, "pe", ("st4", i - 2, NQB - 1))
                                S.mark("dve", ("ye", i, tb, o), em and v.tensor_tensor(
                                    ysq[:, o * 512:(o + 1) * 512],
                                    ybuf[:, o * 2048 + tb * 512:
                                         o * 2048 + tb * 512 + 512],
                                    ybuf[:, o * 2048 + tb * 512:
                                         o * 2048 + tb * 512 + 512], ALU.mult))
                        else:
                            for o in range(NCH):
                                S.wk(v, "pe", ("ymm", i, tb, o))
                                S.mark("dve", ("ye", i, tb, o), em and v.tensor_tensor(
                                    hbuf[:, o * 2048 + tb * 512:
                                         o * 2048 + tb * 512 + 512],
                                    bankA((tb * 8 + o) % 4),
                                    hbuf[:, o * 2048 + tb * 512:
                                         o * 2048 + tb * 512 + 512], ALU.add))
                    if i % 2 == 0:
                        for tb in range(NQB):
                            S.wk(v, "pe", ("st4", i, tb))
                            if tb > 0:
                                S.wk(v, "M2", ("m2v4", i, tb - 1))
                            if em:
                                v.tensor_scalar(out=rows2[0:1, 0:512],
                                                in0=bankB(tb)[0:1, :],
                                                scalar1=1.0 / D, scalar2=None,
                                                op0=ALU.mult)
                                v.tensor_scalar(out=rows2[32:33, 0:512],
                                                in0=bankB(tb)[32:33, :],
                                                scalar1=1.0 / D, scalar2=EPS,
                                                op0=ALU.mult, op1=ALU.add)
                            S.mark("dve", ("rmu4", i, tb), em and v.tensor_mul(
                                rows2[0:1, 512:1024], rows2[0:1, 0:512],
                                rows2[0:1, 0:512]))
                            S.wk(v, "M2", ("m2v4", i, tb))
                            S.mark("dve", ("rw14", i, tb), em and v.tensor_sub(
                                rows[0:1, 0:512], rows[0:1, 512:1024],
                                rows2[0:1, 512:1024]))
                            S.wk(v, "act", ("rows4", i, tb))
                            if tb > 0:
                                S.wk(v, "pe", ("ab4", i, tb - 1))
                            elif i >= 2:
                                S.wk(v, "pe", ("ab4", i - 2, NQB - 1))
                            em and v.tensor_copy(rowsb[0:1, 0:512],
                                                 rows2[0:1, 512:1024])
                            S.mark("dve", ("rw24", i, tb), em and v.scalar_tensor_tensor(
                                out=rowsb[0:1, 512:1024], in0=rows2[0:1, 0:512],
                                scalar=-1.0, in1=rows2[0:1, 512:1024],
                                op0=ALU.mult, op1=ALU.mult))
                        for tb in range(NQB):
                            S.wk(v, "pe", ("ab4", i, tb))
                            for o in range(NCH):
                                if tb > 0:
                                    S.wk(v, "act", ("gel", i, tb - 1, o))
                                em and v.tensor_tensor(
                                    v1t[:, o * 512:(o + 1) * 512],
                                    ybuf[:, o * 2048 + tb * 512:
                                         o * 2048 + tb * 512 + 512],
                                    bankA(tb), ALU.mult)
                                S.mark("dve", ("nm", i, tb, o), em and v.tensor_tensor(
                                    v1t[:, o * 512:(o + 1) * 512],
                                    v1t[:, o * 512:(o + 1) * 512],
                                    bankB(tb), ALU.add))
                    else:
                        if b < 2:
                            for ch in range(NCH):
                                S.mark("dve", ("hbc", b, ch), em and v.tensor_copy(
                                    hbf[:, ch * 2048:(ch + 1) * 2048],
                                    hbuf[:, ch * 2048:(ch + 1) * 2048]))

            def p4_sy(sy):
                em = S.em
                for i in range(6):
                    if debug and i == 0:
                        S.wk(sy, "dve", ("dq", 0, NCH - 1))
                        S.bump("sy", em and sy.dma_start(dbg_w, wTb[:, 0:8192]),
                               dma=True)
                    if debug and i == 0:
                        S.wk(sy, "dve", ("ut", 0, 0))
                        S.bump("sy", em and sy.dma_start(dbg_u, uTa[0:33, :]), dma=True)
                    if i % 2 == 0:
                        for tb in range(NQB):
                            S.wk(sy, "dve", ("rmu4", i, tb))
                            S.mark("M2", ("m2v4", i, tb), em and sy.dma_start(
                                rows[0:1, 512:1024], rows2[32:33, 0:512]), dma=True)
                    if debug and i == 0:
                        S.wk(sy, "pe", ("st4", 0, NQB - 1))
                        S.bump("sy", em and sy.dma_start(dbg_y, ybuf[:]), dma=True)
                    if debug and i == 0:
                        S.wk(sy, "act", ("gel", 0, NQB - 1, NCH - 1))
                        S.bump("sy", em and sy.dma_start(dbg_g, ybuf[:]), dma=True)
                    if debug and i == 1:
                        S.wk(sy, "dve", ("ye", 1, NQB - 1, NCH - 1))
                        S.bump("sy", em and sy.dma_start(dbg_h1, hbuf[:]), dma=True)
                S.wt(sy, "M2")
                if debug and em:
                    sy.wait_ge(sy_q, S.n["sy"])

            def p4_gp(g):
                em = S.em
                S.mark("HLD", ("lat",), em and g.dma_start(laTb[:], laT_bf),
                       dma=True)
                for ch in range(NCH):
                    if ch == NCH - 1:
                        S.mark("HLD", ("hld",), em and g.dma_start(
                            hbuf[:, ch * 2048:(ch + 1) * 2048], hscr[ch]), dma=True)
                    else:
                        S.bump("HLD", em and g.dma_start(
                            hbuf[:, ch * 2048:(ch + 1) * 2048], hscr[ch]), dma=True)
                for i in range(6):
                    if i >= 2:
                        S.wk(g, "pe", ("ymm", i - 2, NQB - 1, NCH - 1))
                    S.mark("LB%d" % (i % 2), ("lb", i), em and g.dma_start(
                        lbb[0:33, (i % 2) * 1024:(i % 2) * 1024 + 1024],
                        lbTa_bf[i]), dma=True)
                    for ch in range(NCH):
                        gqi = i * 8 + ch
                        if gqi >= 2:
                            S.wk(g, "dve", ("dq", (gqi - 2) // 8, (gqi - 2) % 8))
                            S.wk(g, "pe", ("se", (gqi - 2) // 8, (gqi - 2) % 8))
                        par = ch % 2
                        S.mark("QB%d" % par, ("qbld", i, ch), em and g.dma_start(
                            qc[:, par * 1024:par * 1024 + 1024],
                            qwT_in[i, ch]), dma=True)
                        S.mark("SB%d" % par, ("qld", i, ch), em and g.dma_start(
                            sTb[0:8, par * 1024:par * 1024 + 1024],
                            sT_bf[i, :, ch * 1024:(ch + 1) * 1024]), dma=True)

            two_pass(b4, [("tensor", p4_pe), ("scalar", p4_act),
                          ("vector", p4_dve), ("sync", p4_sy), ("gpsimd", p4_gp)])

          _e5 = ExitStack()
          with _e5:
              ofb = _e5.enter_context(nc.sbuf_tensor("ofb", [128, 4096], F32))
              x2s = _e5.enter_context(nc.sbuf_tensor("x2s", [128, 1024], F32))
              otm = _e5.enter_context(nc.sbuf_tensor("otm", [128, 2048], F32))
              b5 = _e5.enter_context(nc.Block())
              def p5_pe(t):
                  em = S.em
                  for tb in range(NQB):
                      for tt in range(4):
                          ins = None
                          for o in range(NCH):
                              if tt == 0:
                                  S.wk(t, "dve", ("ad5", tb, o))
                              if tb > 0 and o == 0:
                                  S.wk(t, "dve", ("oe5", tb - 1, tt))
                              bank = bankA(tt) if o < 4 else bankB(tt)
                              ins = em and t.transpose(
                                  bank[:, (o % 4) * 128:(o % 4) * 128 + 128],
                                  ofb[:, o * 512 + tt * 128:o * 512 + (tt + 1) * 128],
                                  identf[:])
                          S.mark("pe", ("tp5", tb, tt), ins)

              def p5_dve(v):
                  em = S.em
                  for tb in range(NQB):
                      for o in range(NCH):
                          S.wk(v, "XL%d" % (o % 2), ("x2l", tb, o))
                          if tb > 0:
                              S.wk(v, "pe", ("tp5", tb - 1, 3))
                          S.mark("dve", ("ad5", tb, o), em and v.tensor_add(
                              ofb[:, o * 512:(o + 1) * 512],
                              hbuf[:, o * 2048 + tb * 512:o * 2048 + tb * 512 + 512],
                              x2s[:, (o % 2) * 512:(o % 2) * 512 + 512]))
                      for tt in range(4):
                          S.wk(v, "pe", ("tp5", tb, tt))
                          gi = tb * 4 + tt
                          if gi >= 2:
                              S.wk(v, "O5%d" % (tt % 2),
                                   ("ost5", (gi - 2) // 4, (gi - 2) % 4))
                          par = tt % 2
                          em and v.tensor_copy(otm[:, par * 1024:par * 1024 + 512],
                                               bankA(tt))
                          S.mark("dve", ("oe5", tb, tt), em and v.tensor_copy(
                              otm[:, par * 1024 + 512:par * 1024 + 1024],
                              bankB(tt)))

              def p5_sy(sy):
                  em = S.em
                  for tb in range(NQB):
                      for o in range(NCH):
                          if tb > 0 or o >= 2:
                              po = (tb, o - 2) if o >= 2 else (tb - 1, NCH - 2 + o)
                              S.wk(sy, "dve", ("ad5",) + po)
                          S.mark("XL%d" % (o % 2), ("x2l", tb, o), em and sy.dma_start(
                              x2s[:, (o % 2) * 512:(o % 2) * 512 + 512],
                              x2scr[o, :, tb * 512:(tb + 1) * 512]), dma=True)
                      for tt in range(4):
                          S.wk(sy, "dve", ("oe5", tb, tt))
                          S.mark("O5%d" % (tt % 2), ("ost5", tb, tt),
                                 em and sy.dma_start(
                              out_d[(tb * 4 + tt) * 128:(tb * 4 + tt + 1) * 128, :],
                              otm[:, (tt % 2) * 1024:(tt % 2) * 1024 + 1024]),
                              dma=True)
                  for nm in ("O50", "O51", "XL0", "XL1"):
                      S.wt(sy, nm)

              two_pass(b5, [("tensor", p5_pe), ("vector", p5_dve),
                            ("sync", p5_sy)])

    return nc


_NC_CACHE = {}


def _get_nc():
    if "nc" not in _NC_CACHE:
        _NC_CACHE["nc"] = build_nc()
    return _NC_CACHE["nc"]


def kernel_dbg(*a, **k):
    return _kernel_impl(True, *a, **k)


def kernel(*a, **k):
    return _kernel_impl(False, *a, **k)


def _kernel_impl(debug, L0_attn_x, L0_attn_Wq, L0_attn_bq, L0_attn_Wk, L0_attn_bk,
           L0_attn_Wv, L0_attn_bv, L0_attn_Wo, L0_attn_bo,
           L0_ffn_qweights, L0_ffn_scales, L0_ffn_biases,
           L0_ffn_lora_a, L0_ffn_lora_b, L0_ffn_ln_gamma, L0_ffn_ln_beta):
    f32 = np.float32
    x = np.asarray(L0_attn_x, f32)
    Wq = np.asarray(L0_attn_Wq, f32)
    bq = np.asarray(L0_attn_bq, f32)
    Wk = np.asarray(L0_attn_Wk, f32)
    bk = np.asarray(L0_attn_bk, f32)
    Wv = np.asarray(L0_attn_Wv, f32)
    bv = np.asarray(L0_attn_bv, f32)
    Wo = np.asarray(L0_attn_Wo, f32)
    bo = np.asarray(L0_attn_bo, f32)
    qw = np.asarray(L0_ffn_qweights, np.int32)
    sc = np.asarray(L0_ffn_scales, f32)
    fb = np.asarray(L0_ffn_biases, f32)
    la = np.asarray(L0_ffn_lora_a, f32)
    lb = np.asarray(L0_ffn_lora_b, f32)
    lg = np.asarray(L0_ffn_ln_gamma, f32)
    lbe = np.asarray(L0_ffn_ln_beta, f32)

    perm = np.concatenate([np.arange(0, 64, 2), np.arange(1, 64, 2)])
    permg = np.concatenate([h * 64 + perm for h in range(16)])
    Wqp, bqp = Wq[permg], bq[permg]
    Wkp, bkp = Wk[permg], bk[permg]

    wT = np.stack([
        np.ascontiguousarray(Wqp.T).reshape(NCH, 128, D),
        np.ascontiguousarray(Wkp.T).reshape(NCH, 128, D),
        np.ascontiguousarray(Wv.T).reshape(NCH, 128, D),
        np.ascontiguousarray(Wo.T).reshape(NCH, 128, D),
    ]).astype(f32)

    cols = np.zeros((128, 128), f32)
    cols[:, 0:8] = bqp.reshape(8, 128).T
    cols[:, 8:16] = bkp.reshape(8, 128).T
    cols[:, 16:24] = bo.reshape(8, 128).T
    for b in range(3):
        cols[:, 24 + 8 * b:32 + 8 * b] = lg[b].reshape(8, 128).T
        cols[:, 48 + 8 * b:56 + 8 * b] = lbe[b].reshape(8, 128).T

    identf = np.eye(128, dtype=f32)
    emat = np.zeros((8, 128), NPBF)
    for j in range(8):
        emat[j, j * 16:(j + 1) * 16] = np.float32(1.0 / 7.5)

    qwT = np.stack([np.ascontiguousarray(qw[i].T).reshape(NCH, 128, D)
                    for i in range(6)])
    sT = np.zeros((6, 8, NCH * D), f32)
    for i in range(6):
        sTi = np.ascontiguousarray(sc[i].T)
        sT[i] = sTi.reshape(8, 8, D).transpose(1, 0, 2).reshape(8, NCH * D)
    laT = np.stack([np.ascontiguousarray(la[i].T).reshape(NCH, 128, 32)
                    for i in range(6)]).astype(f32)
    lbTa = np.zeros((6, 33, D), f32)
    for i in range(6):
        lbTa[i, 0:32] = lb[i].T
        lbTa[i, 32] = fb[i]

    inv = (10000.0 ** (-np.arange(32) / 32.0)).astype(np.float64)

    def rope_tables(pos):
        ang = pos[None, :].astype(np.float64) * inv[:, None]
        C32 = np.cos(ang).astype(f32)
        S32 = np.sin(ang).astype(f32)
        C = np.concatenate([C32, C32, C32, C32], axis=0).astype(NPBF)
        Sg = np.concatenate([-S32, S32, -S32, S32], axis=0).astype(NPBF)
        return C, Sg

    in_maps = []
    for c in range(8):
        b, h2 = c // 2, c % 2
        own = np.arange(h2 * OT, (h2 + 1) * OT)
        oth = np.arange((1 - h2) * OT, (2 - h2) * OT)
        pos = np.concatenate([own, oth])
        xr = np.ascontiguousarray(x[b][pos])
        C, Sg = rope_tables(pos)
        in_maps.append(dict(
            x_in=xr, wT_in=wT, cols_in=cols, bv_in=bv.reshape(1, D),
            ropeC=np.ascontiguousarray(C), ropeS=np.ascontiguousarray(Sg),
            identf_in=identf, emat_in=emat, qwT_in=qwT, sT_in=sT,
            laT_in=laT, lbTa_in=lbTa,
        ))

    nc = _get_nc() if not debug else build_nc(True)
    res = run_bass_kernel_spmd(nc, in_maps, core_ids=list(range(8)))
    out = np.zeros((4, T, D), f32)
    for c in range(8):
        b, h2 = c // 2, c % 2
        out[b, h2 * OT:(h2 + 1) * OT] = res.results[c]["out"]
    if debug:
        return out, res.results
    return out
